# version 09
# baseline (speedup 1.0000x reference)
import numpy as np
import ml_dtypes
import concourse.bass as bass
import concourse.mybir as mybir
from concourse.bass_utils import run_bass_kernel_spmd
from contextlib import ExitStack

F32 = mybir.dt.float32
BF16 = mybir.dt.bfloat16
AF = mybir.ActivationFunctionType
ALU = mybir.AluOpType
AX = mybir.AxisListType.X

N_CORES = 8
H = 4096
I_TRUE = 11008
IP = 11264
HC = 32
WI = 256
NIC = IP // WI
NPC = IP // 128
NSTAR = 5504.0
C0 = 0.5184071992260827
WARM = (1.0, 0.87, 1.15)
N_IT = 11
ACT_COUNT = True
EPS = 1e-5
CLAMP_D = 0.001


class Sem:
    def __init__(self, handle):
        self.h = handle
        self.n = 0

    def inc(self, instr, k=1):
        instr.then_inc(self.h, k)
        self.n += k
        return self.n


def build_nc(T_core, GA=1024, GD=512, dump=False):
    GA = min(GA, T_core)
    GD = min(GD, GA)
    assert GA % 512 == 0 and GD % 128 == 0
    NT = T_core // 128
    TGA = GA // 128
    TGD = GD // 128
    n_ga = T_core // GA
    n_gd = GA // GD
    n_tch = GA // 512

    nc = bass.Bass("TRN2", target_bir_lowering=False, debug=False)

    x_in = nc.dram_tensor("x", [T_core, H], F32, kind="ExternalInput").ap()
    wg_d = nc.dram_tensor("wg", [NIC, 128, HC * WI], F32, kind="ExternalInput").ap()
    wu_d = nc.dram_tensor("wu", [NIC, 128, HC * WI], F32, kind="ExternalInput").ap()
    wd_d = nc.dram_tensor("wd", [HC, 128, NPC * 128], F32, kind="ExternalInput").ap()
    wb_d = nc.dram_tensor("wb", [HC, 128, HC * 128], F32, kind="ExternalInput").ap()
    a_d = nc.dram_tensor("a_img", [128, HC * 128], F32, kind="ExternalInput").ap()
    b_d = nc.dram_tensor("b_img", [128, HC * 128], F32, kind="ExternalInput").ap()
    wr_d = nc.dram_tensor("wr", [128, HC * 8], F32, kind="ExternalInput").ap()
    bb_d = nc.dram_tensor("bb", [128, HC], F32, kind="ExternalInput").ap()
    idb_d = nc.dram_tensor("identb", [128, 128], BF16, kind="ExternalInput").ap()
    idf_d = nc.dram_tensor("identf", [128, 128], F32, kind="ExternalInput").ap()
    e_d = nc.dram_tensor("eexp", [8, 128], BF16, kind="ExternalInput").ap()

    out_d = nc.dram_tensor("out", [T_core, H], F32, kind="ExternalOutput").ap()

    scr_kind = "ExternalOutput" if dump else "Internal"
    a_scr = nc.dram_tensor("a_scr", [NT, 128, IP], F32, kind=scr_kind).ap()
    u_scr = nc.dram_tensor("u_scr", [NT, 128, IP], BF16, kind=scr_kind).ap()
    g_scr = nc.dram_tensor("g_scr", [n_ga, HC, 128, GA], BF16, kind=scr_kind).ap()
    tau_d = None
    xn_dump = None
    if dump:
        tau_d = nc.dram_tensor("tau_dump", [NT, 128, 1], F32, kind="ExternalOutput").ap()
        xn_dump = nc.dram_tensor("xn_dump", [n_ga, 128, HC * GA], BF16,
                                 kind="ExternalOutput").ap()

    ctx = ExitStack()
    with ctx:
        sb = lambda nm, sh, dt: ctx.enter_context(nc.sbuf_tensor(nm, sh, dt))

        AR_COLS = max(HC * GA, NPC * TGD * 128)
        arena = sb("arena", [128, AR_COLS], BF16)

        WS_COLS = 17408
        ws = sb("ws", [128, WS_COLS], F32)
        ws_bf = ws[:].bitcast(BF16)

        wg_buf = [ws_bf[:, 0:8192], ws_bf[:, 8192:16384]]
        wu_buf = [ws_bf[:, 16384:24576], ws_bf[:, 24576:32768]]
        x_sb = [ws[:, 0:H], ws[:, H:2 * H]]
        xn_tm = [ws_bf[:, 16384:16384 + H], ws_bf[:, 16384 + H:16384 + 2 * H]]
        junk_ln = ws_bf[:, 16384 + 2 * H:16384 + 3 * H]
        wb_buf = [ws_bf[:, 0:4096], ws_bf[:, 4096:8192]]
        a_sb = ws[:, 0:IP]
        junk_sel = ws_bf[:, 2 * IP:3 * IP]
        wd_buf = [ws_bf[:, 0:IP], ws_bf[:, IP:2 * IP]]

        A_sb = sb("A_sb", [128, HC * 128], BF16)
        B_sb = sb("B_sb", [128, HC * 128], BF16)
        Wr_sb = sb("Wr_sb", [128, HC * 8], BF16)
        bb_sb = sb("bb_sb", [128, HC], F32)
        idb_sb = sb("idb_sb", [128, 128], BF16)
        idf_sb = sb("idf_sb", [128, 128], F32)
        e_sb = sb("e_sb", [8, 128], BF16)
        eps_sb = sb("eps_sb", [128, 1], F32)

        NS = 3
        sgn_st = [sb(f"sgn{j}", [128, WI], BF16) for j in range(NS)]
        s_st = [sb(f"s{j}", [128, WI], F32) for j in range(NS)]
        aa_st = [sb(f"aa{j}", [128, WI], F32) for j in range(NS)]
        uu_st = [sb(f"uu{j}", [128, WI], BF16) for j in range(NS)]
        m1_parts = sb("m1p", [128, NT * NIC], F32)
        g_st = [sb(f"g{j}", [128, 512], BF16) for j in range(2)]
        gz_st = [sb(f"gz{j}", [128, 512], BF16) for j in range(2)]
        z_st = [sb(f"z{j}", [128, 512], F32) for j in range(2)]
        x_st = [sb(f"xst{j}", [128, 128], F32) for j in range(4)]
        o_st = [sb(f"ost{j}", [128, 128], F32) for j in range(4)]
        p_st5 = [sb(f"pst{j}", [128, 512], BF16) for j in range(2)]
        w_tm = sb("w_tm", [128, 128], F32)
        lg_sb = sb("lg", [128, 8], F32)
        m8_sb = sb("m8", [128, 8], F32)
        eq_sb = sb("eq", [128, 8], F32)
        wt2_sb = sb("wt2", [128, 8], F32)
        w_fm = sb("w_fm", [8, GA], BF16)
        w_exp = sb("w_exp", [128, GA], BF16)
        hw_sb = sb("hw", [128, GA], BF16)
        st_ = {}
        for nm in ["sumx", "mean", "sumsq", "mean2", "var", "std", "rstd",
                   "negmr", "m1", "tau0", "lo", "hi", "nlo", "nhi", "mid",
                   "cnt", "pred", "npred", "t0", "t1", "u0", "d_", "dn",
                   "rr", "f0", "e0", "d21", "e2", "den", "rw1", "rw2"]:
            st_[nm] = sb("sc_" + nm, [128, 1], F32)
        pred_u8 = sb("pred_u8", [128, 1], mybir.dt.uint8)

        base_names = ["qG", "qS", "ePE", "eACT", "eDVE", "qSEL", "qD"]
        slot_specs = [("qW", 2), ("qX", 2), ("qA", 3), ("qU", 3),
                      ("qGs", 2), ("qGz", 2), ("qXc", 4), ("qO", 4)]
        sems = {nm: Sem(ctx.enter_context(nc.semaphore(nm))) for nm in base_names}
        for nm, k in slot_specs:
            for j in range(k):
                sems[f"{nm}{j}"] = Sem(ctx.enter_context(nc.semaphore(f"{nm}{j}")))
        qG, qS, ePE, eACT, eDVE = (sems[k] for k in ["qG", "qS", "ePE", "eACT", "eDVE"])
        qSEL, qD = sems["qSEL"], sems["qD"]
        qW = [sems[f"qW{j}"] for j in range(2)]
        qX = [sems[f"qX{j}"] for j in range(2)]
        qA = [sems[f"qA{j}"] for j in range(3)]
        qU = [sems[f"qU{j}"] for j in range(3)]
        qGs = [sems[f"qGs{j}"] for j in range(2)]
        qGz = [sems[f"qGz{j}"] for j in range(2)]
        qXc = [sems[f"qXc{j}"] for j in range(4)]
        qO = [sems[f"qO{j}"] for j in range(4)]

        block = ctx.enter_context(nc.Block())

        snap = {k: 0 for k in sems}

        def take_snap():
            for k in sems:
                snap[k] = sems[k].n

        def bar(eng):
            for nm in sems:
                if snap[nm] > 0:
                    eng.wait_ge(sems[nm].h, snap[nm])

        def settle(eng, instr):
            eDVE.inc(instr)
            eng.wait_ge(eDVE.h, eDVE.n)

        def init_phase():
            @block.gpsimd
            def _(g):
                g.dma_start(out=A_sb[:], in_=a_d).then_inc(qG.h, 16)
                g.dma_start(out=B_sb[:], in_=b_d).then_inc(qG.h, 16)
                g.dma_start(out=Wr_sb[:], in_=wr_d).then_inc(qG.h, 16)
                qG.n += 48

            @block.sync
            def _(s):
                s.dma_start(out=bb_sb[:], in_=bb_d).then_inc(qS.h, 16)
                s.dma_start(out=idb_sb[:], in_=idb_d).then_inc(qS.h, 16)
                s.dma_start(out=idf_sb[:], in_=idf_d).then_inc(qS.h, 16)
                s.dma_start(out=e_sb[:], in_=e_d).then_inc(qS.h, 16)
                qS.n += 64

            @block.vector
            def _(v):
                v.memset(eps_sb[:], EPS)
                i = v.memset(w_tm[:], 0.0)
                eDVE.inc(i)
            take_snap()

        def phase_ln(ga):
            PE0, ACT0, DVE0 = ePE.n, eACT.n, eDVE.n
            xtoks = [0] * TGA
            with ExitStack() as pctx:
                ptp = [pctx.enter_context(nc.psum_tensor(f"pln{ga}_{j}", [128, 1024], BF16))
                       for j in range(2)]
                for tt in range(TGA):
                    t_abs = ga * TGA + tt
                    r0 = t_abs * 128

                    @block.sync
                    def _(s, tt=tt, r0=r0):
                        if tt == 0:
                            bar(s)
                        if tt >= 2:
                            s.wait_ge(eACT.h, ACT0 + 2 * (tt - 2) + 2)
                        i = s.dma_start(out=x_sb[tt % 2], in_=x_in[r0:r0 + 128, :])
                        xtoks[tt] = qX[tt % 2].inc(i, 16)

                    @block.vector
                    def _(v, tt=tt):
                        if tt == 0:
                            bar(v)
                        v.wait_ge(qX[tt % 2].h, xtoks[tt])
                        xs = x_sb[tt % 2]
                        settle(v, v.reduce_sum(st_["sumx"][:], xs, axis=AX))
                        settle(v, v.tensor_scalar(out=st_["mean"][:], in0=st_["sumx"][:],
                                                  scalar1=1.0 / H, scalar2=None,
                                                  op0=ALU.mult))
                        settle(v, v.scalar_tensor_tensor(out=junk_ln, in0=xs, scalar=1.0,
                                                         in1=xs, op0=ALU.mult,
                                                         op1=ALU.mult,
                                                         accum_out=st_["sumsq"][:]))
                        settle(v, v.tensor_tensor(out=st_["mean2"][:], in0=st_["mean"][:],
                                                  in1=st_["mean"][:], op=ALU.mult))
                        settle(v, v.tensor_scalar(out=st_["var"][:], in0=st_["sumsq"][:],
                                                  scalar1=1.0 / H, scalar2=None,
                                                  op0=ALU.mult))
                        settle(v, v.tensor_tensor(out=st_["var"][:], in0=st_["var"][:],
                                                  in1=st_["mean2"][:], op=ALU.subtract))
                        i = v.tensor_copy(out=st_["sumx"][:], in_=st_["var"][:])
                        eDVE.inc(i)

                    @block.scalar
                    def _(s, tt=tt):
                        if tt == 0:
                            bar(s)
                        s.wait_ge(eDVE.h, DVE0 + 43 * tt + 7)
                        s.activation(out=st_["std"][:], in_=st_["var"][:],
                                     func=AF.Sqrt, bias=eps_sb[:])
                        i = s.activation(out=st_["u0"][:], in_=st_["std"][:],
                                         func=AF.Identity)
                        eACT.inc(i)

                    @block.vector
                    def _(v, tt=tt):
                        v.wait_ge(eACT.h, ACT0 + 2 * tt + 1)
                        settle(v, v.reciprocal(st_["rstd"][:], st_["std"][:]))
                        settle(v, v.tensor_tensor(out=st_["negmr"][:], in0=st_["mean"][:],
                                                  in1=st_["rstd"][:], op=ALU.mult))
                        settle(v, v.tensor_scalar(out=st_["negmr"][:], in0=st_["negmr"][:],
                                                  scalar1=-1.0, scalar2=None, op0=ALU.mult))
                        i = v.tensor_copy(out=st_["mean2"][:], in_=st_["negmr"][:])
                        eDVE.inc(i)

                    @block.scalar
                    def _(s, tt=tt):
                        s.wait_ge(eDVE.h, DVE0 + 43 * tt + 11)
                        i = s.activation(out=xn_tm[tt % 2], in_=x_sb[tt % 2],
                                         func=AF.Identity, bias=st_["negmr"][:],
                                         scale=st_["rstd"][:])
                        eACT.inc(i)

                    @block.tensor
                    def _(t, tt=tt):
                        if tt == 0:
                            bar(t)
                        t.wait_ge(eACT.h, ACT0 + 2 * tt + 2)
                        for hb in range(HC):
                            k = tt * HC + hb
                            if k >= 2:
                                kc = k - 2
                                t.wait_ge(eDVE.h,
                                          DVE0 + 43 * (kc // HC) + 12 + (kc % HC))
                            i = t.transpose(ptp[k % 2][:, 0:128],
                                            in_=xn_tm[tt % 2][:, hb * 128:(hb + 1) * 128],
                                            identity=idb_sb[:])
                            ePE.inc(i)

                    @block.vector
                    def _(v, tt=tt):
                        for hb in range(HC):
                            k = tt * HC + hb
                            v.wait_ge(ePE.h, PE0 + k + 1)
                            col = hb * GA + tt * 128
                            i = v.tensor_copy(out=arena[:, col:col + 128],
                                              in_=ptp[k % 2][:, 0:128])
                            eDVE.inc(i)
                assert eDVE.n == DVE0 + 43 * TGA
                assert eACT.n == ACT0 + 2 * TGA
                assert ePE.n == PE0 + HC * TGA
            take_snap()

        def phase_gu(ga):
            PE0, ACT0, DVE0 = ePE.n, eACT.n, eDVE.n
            wtoks = [0] * NIC
            atoks = [0] * (NIC * TGA)
            utoks = [0] * (NIC * TGA)
            with ExitStack() as pctx:
                pgu = [pctx.enter_context(nc.psum_tensor(f"pgu{ga}_{t}", [128, 512], F32))
                       for t in range(TGA)]
                for ic in range(NIC):
                    @block.gpsimd
                    def _(g, ic=ic):
                        if ic == 0:
                            bar(g)
                        if ic >= 2:
                            g.wait_ge(ePE.h, PE0 + (ic - 1) * TGA)
                        qW[ic % 2].inc(g.dma_start(out=wg_buf[ic % 2], in_=wg_d[ic]), 16)
                        i = g.dma_start(out=wu_buf[ic % 2], in_=wu_d[ic])
                        wtoks[ic] = qW[ic % 2].inc(i, 16)

                    @block.tensor
                    def _(t, ic=ic):
                        if ic == 0:
                            bar(t)
                        t.wait_ge(qW[ic % 2].h, wtoks[ic])
                        for tt in range(TGA):
                            if ic >= 1:
                                kprev = (ic - 1) * TGA + tt
                                t.wait_ge(eACT.h, ACT0 + 2 * kprev + 1)
                                t.wait_ge(eDVE.h, DVE0 + kprev + 1)
                            for hc in range(HC):
                                t.matmul(pgu[tt][:, 0:WI],
                                         lhsT=arena[:, hc * GA + tt * 128:hc * GA + tt * 128 + 128],
                                         rhs=wg_buf[ic % 2][:, hc * WI:(hc + 1) * WI],
                                         start=(hc == 0), stop=(hc == HC - 1))
                            for hc in range(HC):
                                i = t.matmul(pgu[tt][:, WI:2 * WI],
                                             lhsT=arena[:, hc * GA + tt * 128:hc * GA + tt * 128 + 128],
                                             rhs=wu_buf[ic % 2][:, hc * WI:(hc + 1) * WI],
                                             start=(hc == 0), stop=(hc == HC - 1))
                            ePE.inc(i)

                    @block.scalar
                    def _(s, ic=ic):
                        if ic == 0:
                            bar(s)
                        for tt in range(TGA):
                            k = ic * TGA + tt
                            t_abs = ga * TGA + tt
                            s.wait_ge(ePE.h, PE0 + k + 1)
                            if k >= NS:
                                s.wait_ge(eDVE.h, DVE0 + (k - NS) + 1)
                                s.wait_ge(qA[(k - NS) % NS].h, atoks[k - NS])
                            sl = k % NS
                            s.activation(out=sgn_st[sl][:], in_=pgu[tt][:, 0:WI],
                                         func=AF.Sign)
                            i = s.activation(out=s_st[sl][:], in_=pgu[tt][:, 0:WI],
                                             func=AF.Silu)
                            eACT.inc(i)
                            i = s.activation(out=aa_st[sl][:], in_=s_st[sl][:],
                                             func=AF.Abs,
                                             accum_out=m1_parts[:, t_abs * NIC + ic:
                                                               t_abs * NIC + ic + 1])
                            eACT.inc(i)

                    @block.vector
                    def _(v, ic=ic):
                        if ic == 0:
                            bar(v)
                        for tt in range(TGA):
                            k = ic * TGA + tt
                            sl = k % NS
                            v.wait_ge(eACT.h, ACT0 + 2 * k + 1)
                            if k >= NS:
                                v.wait_ge(qU[(k - NS) % NS].h, utoks[k - NS])
                            i = v.tensor_tensor(out=uu_st[sl][:],
                                                in0=pgu[tt][:, WI:2 * WI],
                                                in1=sgn_st[sl][:], op=ALU.mult)
                            eDVE.inc(i)

                    @block.sync
                    def _(s, ic=ic):
                        if ic == 0:
                            bar(s)
                        for tt in range(TGA):
                            k = ic * TGA + tt
                            t_abs = ga * TGA + tt
                            sl = k % NS
                            s.wait_ge(eACT.h, ACT0 + 2 * k + 2)
                            i = s.dma_start(out=a_scr[t_abs, :, ic * WI:(ic + 1) * WI],
                                            in_=aa_st[sl][:])
                            atoks[k] = qA[k % NS].inc(i, 16)
                            s.wait_ge(eDVE.h, DVE0 + k + 1)
                            i = s.dma_start(out=u_scr[t_abs, :, ic * WI:(ic + 1) * WI],
                                            in_=uu_st[sl][:])
                            utoks[k] = qU[k % NS].inc(i, 16)
                assert ePE.n == PE0 + NIC * TGA
                assert eACT.n == ACT0 + 2 * NIC * TGA
                assert eDVE.n == DVE0 + NIC * TGA
            take_snap()

        def phase_ffn(ga):
            PE0, ACT0, DVE0 = ePE.n, eACT.n, eDVE.n
            wbtoks = [0] * HC
            gstoks = [0] * (HC * n_tch)
            with ExitStack() as pctx:
                pb = [pctx.enter_context(nc.psum_tensor(f"pb{ga}_{j}", [128, 512], F32))
                      for j in range(2)]
                phh = pctx.enter_context(nc.psum_tensor(f"phh{ga}", [128, 512], F32))
                pr = pctx.enter_context(nc.psum_tensor(f"pr{ga}", [128, 512], F32))
                pwt = pctx.enter_context(nc.psum_tensor(f"pwt{ga}", [128, 512], F32))
                pex = pctx.enter_context(nc.psum_tensor(f"pex{ga}", [128, 512], F32))

                for tt in range(TGA):
                    @block.tensor
                    def _(t, tt=tt):
                        if tt == 0:
                            bar(t)
                        else:
                            t.wait_ge(eDVE.h, DVE0 + 13 * tt)
                        for hc in range(HC):
                            i = t.matmul(pr[:, 0:8],
                                         lhsT=arena[:, hc * GA + tt * 128:hc * GA + tt * 128 + 128],
                                         rhs=Wr_sb[:, hc * 8:(hc + 1) * 8],
                                         start=(hc == 0), stop=(hc == HC - 1))
                        ePE.inc(i)

                    @block.vector
                    def _(v, tt=tt):
                        if tt == 0:
                            bar(v)
                        v.wait_ge(ePE.h, PE0 + 2 * tt + 1)
                        settle(v, v.tensor_copy(out=lg_sb[:], in_=pr[:, 0:8]))
                        settle(v, v.max(out=m8_sb[:], in_=lg_sb[:]))
                        settle(v, v.tensor_tensor(out=st_["d21"][:], in0=m8_sb[:, 1:2],
                                                  in1=m8_sb[:, 0:1], op=ALU.subtract))
                        i = v.tensor_copy(out=st_["t0"][:], in_=st_["d21"][:])
                        eDVE.inc(i)

                    @block.scalar
                    def _(s, tt=tt):
                        if tt == 0:
                            bar(s)
                        s.wait_ge(eDVE.h, DVE0 + 13 * tt + 4)
                        s.activation(out=st_["e2"][:], in_=st_["d21"][:], func=AF.Exp)
                        i = s.activation(out=st_["u0"][:], in_=st_["e2"][:],
                                         func=AF.Identity)
                        eACT.inc(i)

                    @block.vector
                    def _(v, tt=tt):
                        v.wait_ge(eACT.h, ACT0 + tt + 1)
                        if tt >= 1:
                            v.wait_ge(ePE.h, PE0 + 2 * tt)
                        settle(v, v.tensor_scalar(out=st_["den"][:], in0=st_["e2"][:],
                                                  scalar1=1.0, scalar2=None, op0=ALU.add))
                        settle(v, v.reciprocal(st_["rw1"][:], st_["den"][:]))
                        settle(v, v.tensor_scalar(out=st_["rw2"][:], in0=st_["rw1"][:],
                                                  scalar1=-1.0, scalar2=1.0,
                                                  op0=ALU.mult, op1=ALU.add))
                        settle(v, v.tensor_tensor(out=eq_sb[:], in0=lg_sb[:],
                                                  in1=m8_sb[:, 0:1].to_broadcast([128, 8]),
                                                  op=ALU.is_equal))
                        settle(v, v.tensor_tensor(out=wt2_sb[:], in0=eq_sb[:],
                                                  in1=st_["rw1"][:].to_broadcast([128, 8]),
                                                  op=ALU.mult))
                        settle(v, v.tensor_tensor(out=eq_sb[:], in0=lg_sb[:],
                                                  in1=m8_sb[:, 1:2].to_broadcast([128, 8]),
                                                  op=ALU.is_equal))
                        settle(v, v.tensor_tensor(out=eq_sb[:], in0=eq_sb[:],
                                                  in1=st_["rw2"][:].to_broadcast([128, 8]),
                                                  op=ALU.mult))
                        i = v.tensor_tensor(out=w_tm[:, 0:8], in0=eq_sb[:],
                                            in1=wt2_sb[:], op=ALU.add)
                        eDVE.inc(i)

                    @block.tensor
                    def _(t, tt=tt):
                        t.wait_ge(eDVE.h, DVE0 + 13 * tt + 12)
                        i = t.transpose(pwt[:, 0:128], in_=w_tm[:], identity=idf_sb[:])
                        ePE.inc(i)

                    @block.vector
                    def _(v, tt=tt):
                        v.wait_ge(ePE.h, PE0 + 2 * tt + 2)
                        i = v.tensor_copy(out=w_fm[:, tt * 128:(tt + 1) * 128],
                                          in_=pwt[0:8, 0:128])
                        eDVE.inc(i)

                for tch in range(n_tch):
                    @block.tensor
                    def _(t, tch=tch):
                        t.wait_ge(eDVE.h, DVE0 + 13 * (4 * tch + 4))
                        if tch >= 1:
                            t.wait_ge(eDVE.h, DVE0 + 13 * TGA + 2 * tch)
                        i = t.matmul(pex[:, 0:512], lhsT=e_sb[:],
                                     rhs=w_fm[:, tch * 512:(tch + 1) * 512],
                                     start=True, stop=True)
                        ePE.inc(i)
                        for hc in range(HC):
                            i = t.matmul(phh[:, 0:512],
                                         lhsT=A_sb[:, hc * 128:(hc + 1) * 128],
                                         rhs=arena[:, hc * GA + tch * 512:
                                                   hc * GA + (tch + 1) * 512],
                                         start=(hc == 0), stop=(hc == HC - 1))
                        ePE.inc(i)

                    @block.vector
                    def _(v, tch=tch):
                        v.wait_ge(ePE.h, PE0 + 2 * TGA + 2 * tch + 1)
                        i = v.tensor_copy(out=w_exp[:, tch * 512:(tch + 1) * 512],
                                          in_=pex[:, 0:512])
                        eDVE.inc(i)
                        v.wait_ge(ePE.h, PE0 + 2 * TGA + 2 * tch + 2)
                        i = v.scalar_tensor_tensor(out=hw_sb[:, tch * 512:(tch + 1) * 512],
                                                   in0=phh[:, 0:512], scalar=2.0,
                                                   in1=w_exp[:, tch * 512:(tch + 1) * 512],
                                                   op0=ALU.mult, op1=ALU.mult)
                        eDVE.inc(i)

                DVEH = eDVE.n
                PEH = PE0 + 2 * TGA + 2 * n_tch

                for oc in range(HC):
                    @block.gpsimd
                    def _(g, oc=oc):
                        if oc == 0:
                            bar(g)
                        if oc >= 2:
                            g.wait_ge(ePE.h, PEH + (oc - 1) * n_tch)
                        i = g.dma_start(out=wb_buf[oc % 2], in_=wb_d[oc])
                        wbtoks[oc] = qW[oc % 2].inc(i, 16)

                    @block.tensor
                    def _(t, oc=oc):
                        t.wait_ge(qW[oc % 2].h, wbtoks[oc])
                        if oc == 0:
                            t.wait_ge(eDVE.h, DVEH)
                        for tch in range(n_tch):
                            j = oc * n_tch + tch
                            if j >= 2:
                                t.wait_ge(eACT.h, ACT0 + TGA + (j - 1))
                            for hc in range(HC):
                                t.matmul(pb[j % 2][:, 0:512],
                                         lhsT=wb_buf[oc % 2][:, hc * 128:(hc + 1) * 128],
                                         rhs=arena[:, hc * GA + tch * 512:
                                                   hc * GA + (tch + 1) * 512],
                                         start=(hc == 0), stop=False)
                            i = t.matmul(pb[j % 2][:, 0:512],
                                         lhsT=B_sb[:, oc * 128:(oc + 1) * 128],
                                         rhs=hw_sb[:, tch * 512:(tch + 1) * 512],
                                         start=False, stop=True)
                            ePE.inc(i)

                    @block.scalar
                    def _(s, oc=oc):
                        for tch in range(n_tch):
                            j = oc * n_tch + tch
                            s.wait_ge(ePE.h, PEH + j + 1)
                            if j >= 2:
                                s.wait_ge(qGs[j % 2].h, gstoks[j - 2])
                            i = s.activation(out=g_st[j % 2][:], in_=pb[j % 2][:, 0:512],
                                             func=AF.Silu, bias=bb_sb[:, oc:oc + 1])
                            eACT.inc(i)

                    @block.sync
                    def _(s, oc=oc):
                        if oc == 0:
                            bar(s)
                        for tch in range(n_tch):
                            j = oc * n_tch + tch
                            s.wait_ge(eACT.h, ACT0 + TGA + j + 1)
                            i = s.dma_start(out=g_scr[ga, oc, :, tch * 512:(tch + 1) * 512],
                                            in_=g_st[j % 2][:])
                            gstoks[j] = qGs[j % 2].inc(i, 16)
                assert ePE.n == PEH + HC * n_tch
                assert eACT.n == ACT0 + TGA + HC * n_tch
                assert eDVE.n == DVE0 + 13 * TGA + 2 * n_tch
            take_snap()

        def phase_sel(ga, gd):
            for td in range(TGD):
                t_abs = ga * TGA + gd * TGD + td
                PE0, ACT0 = ePE.n, eACT.n
                atok_box = [0]

                @block.sync
                def _(s, t_abs=t_abs):
                    bar(s)
                    i = s.dma_start(out=a_sb, in_=a_scr[t_abs])
                    atok_box[0] = qSEL.inc(i, 16)

                with ExitStack() as pctx:
                    pjnk = pctx.enter_context(
                        nc.psum_tensor(f"pjnk{t_abs}", [128, 3072], F32))
                    ptpb = [pctx.enter_context(
                        nc.psum_tensor(f"ptpb{t_abs}_{j}", [128, 1024], BF16))
                        for j in range(2)]

                    mid_toks = []
                    cnt_toks = []

                    @block.vector
                    def _(v, t_abs=t_abs):
                        bar(v)
                        v.wait_ge(qSEL.h, atok_box[0])
                        S = lambda i: settle(v, i)
                        ts, tt_, rc, sel = (v.tensor_scalar, v.tensor_tensor,
                                            v.reciprocal, v.select)
                        v.memset(st_["lo"][:], 0.0)
                        v.memset(st_["nlo"][:], float(I_TRUE))
                        S(v.memset(st_["nhi"][:], 0.0))
                        S(v.reduce_sum(st_["m1"][:],
                                       m1_parts[:, t_abs * NIC:(t_abs + 1) * NIC],
                                       axis=AX))
                        S(ts(out=st_["tau0"][:], in0=st_["m1"][:],
                             scalar1=C0 / I_TRUE, scalar2=None, op0=ALU.mult))
                        S(ts(out=st_["hi"][:], in0=st_["tau0"][:],
                             scalar1=16.0, scalar2=None, op0=ALU.mult))
                        for it in range(N_IT):
                            ill = it >= len(WARM)
                            if it < len(WARM):
                                ts(out=st_["mid"][:], in0=st_["tau0"][:],
                                   scalar1=WARM[it], scalar2=None, op0=ALU.mult)
                                if ill or True:
                                    pass
                                S(ts(out=st_["t1"][:], in0=st_["hi"][:],
                                     scalar1=0.99999, scalar2=None, op0=ALU.mult))
                            else:
                                tt_(out=st_["d_"][:], in0=st_["hi"][:],
                                    in1=st_["lo"][:], op=ALU.subtract)
                                tt_(out=st_["dn"][:], in0=st_["nlo"][:],
                                    in1=st_["nhi"][:], op=ALU.subtract)
                                ts(out=st_["f0"][:], in0=st_["nlo"][:],
                                   scalar1=-NSTAR, scalar2=None, op0=ALU.add)
                                S(ts(out=st_["t1"][:], in0=st_["hi"][:],
                                     scalar1=0.99999, scalar2=None, op0=ALU.mult))
                                S(rc(st_["rr"][:], st_["dn"][:]))
                                S(tt_(out=st_["f0"][:], in0=st_["f0"][:],
                                      in1=st_["d_"][:], op=ALU.mult))
                                S(tt_(out=st_["f0"][:], in0=st_["f0"][:],
                                      in1=st_["rr"][:], op=ALU.mult))
                                S(tt_(out=st_["mid"][:], in0=st_["lo"][:],
                                      in1=st_["f0"][:], op=ALU.add))
                            i = tt_(out=st_["mid"][:], in0=st_["mid"][:],
                                    in1=st_["t1"][:], op=ALU.min)
                            eDVE.inc(i)
                            v.wait_ge(eDVE.h, eDVE.n)
                            i = v.tensor_copy(out=st_["t0"][:], in_=st_["mid"][:])
                            eDVE.inc(i)
                            mid_toks.append(eDVE.n)
                            hi_cut = 5632 if ACT_COUNT else IP
                            i = ts(out=junk_sel[:, 0:hi_cut], in0=a_sb[:, 0:hi_cut],
                                   scalar1=st_["mid"][:], scalar2=0.0,
                                   op0=ALU.is_gt, op1=ALU.add,
                                   accum_out=st_["cnt"][:])
                            eDVE.inc(i)
                            cnt_toks.append(eDVE.n)
                            v.wait_ge(eDVE.h, eDVE.n)
                            if ACT_COUNT:
                                v.wait_ge(eACT.h, ACT0 + it + 1)
                                S(tt_(out=st_["t0"][:], in0=st_["d21"][:],
                                      in1=st_["e2"][:], op=ALU.add))
                                S(ts(out=st_["t0"][:], in0=st_["t0"][:],
                                     scalar1=-0.5, scalar2=2816.0,
                                     op0=ALU.mult, op1=ALU.add))
                                S(tt_(out=st_["cnt"][:], in0=st_["cnt"][:],
                                      in1=st_["t0"][:], op=ALU.add))
                            S(ts(out=pred_u8[:], in0=st_["cnt"][:],
                                 scalar1=NSTAR, scalar2=None, op0=ALU.is_gt))
                            if ill:
                                ts(out=st_["u0"][:], in0=st_["nlo"][:],
                                   scalar1=0.5, scalar2=NSTAR * 0.5,
                                   op0=ALU.mult, op1=ALU.add)
                                i = ts(out=st_["t1"][:], in0=st_["nhi"][:],
                                       scalar1=0.5, scalar2=NSTAR * 0.5,
                                       op0=ALU.mult, op1=ALU.add)
                                eDVE.inc(i)
                                v.wait_ge(eDVE.h, eDVE.n)
                            else:
                                v.tensor_copy(out=st_["u0"][:], in_=st_["nlo"][:])
                                i = v.tensor_copy(out=st_["t1"][:], in_=st_["nhi"][:])
                                eDVE.inc(i)
                                v.wait_ge(eDVE.h, eDVE.n)
                            sel(out=st_["lo"][:], mask=pred_u8[:],
                                on_true=st_["mid"][:], on_false=st_["lo"][:])
                            sel(out=st_["hi"][:], mask=pred_u8[:],
                                on_true=st_["hi"][:], on_false=st_["mid"][:])
                            sel(out=st_["nlo"][:], mask=pred_u8[:],
                                on_true=st_["cnt"][:], on_false=st_["u0"][:])
                            i = sel(out=st_["nhi"][:], mask=pred_u8[:],
                                    on_true=st_["t1"][:], on_false=st_["cnt"][:])
                            eDVE.inc(i)
                            v.wait_ge(eDVE.h, eDVE.n)

                    itend_dve = eDVE.n

                    @block.scalar
                    def _(s, t_abs=t_abs):
                        bar(s)
                        if not ACT_COUNT:
                            return
                        s.wait_ge(qSEL.h, atok_box[0])
                        for it in range(N_IT):
                            s.wait_ge(eDVE.h, mid_toks[it])
                            s.activation(out=pjnk[:, 0:2816],
                                         in_=a_sb[:, 5632:8448], func=AF.Sign,
                                         bias=st_["mid"][:], scale=-1.0,
                                         accum_out=st_["d21"][:])
                            s.activation(out=pjnk[:, 0:2816],
                                         in_=a_sb[:, 8448:11264], func=AF.Sign,
                                         bias=st_["mid"][:], scale=-1.0,
                                         accum_out=st_["e2"][:])
                            i = s.activation(out=st_["den"][:], in_=st_["e2"][:],
                                             func=AF.Identity)
                            eACT.inc(i)

                    if dump:
                        @block.sync
                        def _(s, t_abs=t_abs):
                            s.wait_ge(eDVE.h, itend_dve)
                            i = s.dma_start(out=tau_d[t_abs], in_=st_["hi"][:])
                            qD.inc(i, 16)

                    uptok_box = [0]

                    @block.sync
                    def _(s, t_abs=t_abs):
                        s.wait_ge(eDVE.h, cnt_toks[-1])
                        if ACT_COUNT:
                            s.wait_ge(eACT.h, ACT0 + N_IT)
                        i = s.dma_start(out=junk_sel, in_=u_scr[t_abs])
                        uptok_box[0] = qSEL.inc(i, 16)

                    NCH = NPC // 4
                    tt512 = [0] * NCH

                    @block.vector
                    def _(v, td=td):
                        v.wait_ge(qSEL.h, uptok_box[0])
                        for c in range(NCH):
                            if c >= 2:
                                v.wait_ge(ePE.h, PE0 + 4 * (c - 1))
                            v.scalar_tensor_tensor(out=z_st[c % 2][:],
                                                   in0=a_sb[:, c * 512:(c + 1) * 512],
                                                   scalar=st_["hi"][:],
                                                   in1=a_sb[:, c * 512:(c + 1) * 512],
                                                   op0=ALU.is_gt, op1=ALU.mult)
                            i = v.tensor_tensor(out=p_st5[c % 2][:],
                                                in0=z_st[c % 2][:],
                                                in1=junk_sel[:, c * 512:(c + 1) * 512],
                                                op=ALU.mult)
                            tt512[c] = eDVE.inc(i)

                    ACTP = eACT.n

                    @block.tensor
                    def _(t, td=td):
                        bar(t)
                        for c in range(NCH):
                            t.wait_ge(eDVE.h, tt512[c])
                            for q in range(4):
                                k = 4 * c + q
                                if k >= 2:
                                    t.wait_ge(eACT.h, ACTP + k - 1)
                                i = t.transpose(ptpb[k % 2][:, 0:128],
                                                in_=p_st5[c % 2][:, q * 128:(q + 1) * 128],
                                                identity=idb_sb[:])
                                ePE.inc(i)

                    @block.scalar
                    def _(s, td=td):
                        for k in range(NPC):
                            s.wait_ge(ePE.h, PE0 + k + 1)
                            col = k * (TGD * 128) + td * 128
                            i = s.activation(out=arena[:, col:col + 128],
                                             in_=ptpb[k % 2][:, 0:128], func=AF.Identity)
                            eACT.inc(i)
                    assert ePE.n == PE0 + NPC
                    assert eACT.n == ACTP + NPC
                take_snap()

        def phase_down(ga, gd):
            PE0, DVE0 = ePE.n, eDVE.n
            r_base = (ga * TGA + gd * TGD) * 128
            pe_per = 1 + TGD
            dve_per = 1 + TGD
            gztoks = [0] * HC
            wdtoks = [0] * HC
            xctoks = [0] * (HC * TGD)
            otoks = [0] * (HC * TGD)

            def tok_oadd(k):
                return DVE0 + (k // TGD) * dve_per + 1 + (k % TGD) + 1

            with ExitStack() as pctx:
                pz = [pctx.enter_context(nc.psum_tensor(f"pz{ga}_{gd}_{j}", [128, 512], F32))
                      for j in range(2)]
                ptz = [pctx.enter_context(nc.psum_tensor(f"ptz{ga}_{gd}_{j}", [128, 512], F32))
                       for j in range(2)]
                for hc in range(HC):
                    @block.gpsimd
                    def _(g, hc=hc):
                        if hc == 0:
                            bar(g)
                        if hc >= 2:
                            g.wait_ge(ePE.h, PE0 + (hc - 1) * pe_per)
                        i = g.dma_start(out=wd_buf[hc % 2], in_=wd_d[hc])
                        wdtoks[hc] = qW[hc % 2].inc(i, 16)

                    @block.sync
                    def _(s, hc=hc):
                        if hc == 0:
                            bar(s)
                        if hc >= 2:
                            s.wait_ge(eDVE.h, DVE0 + (hc - 1) * dve_per)
                        i = s.dma_start(out=gz_st[hc % 2][:],
                                        in_=g_scr[ga, hc, :, gd * 512:(gd + 1) * 512])
                        gztoks[hc] = qGz[hc % 2].inc(i, 16)
                        for td in range(TGD):
                            k = hc * TGD + td
                            if k >= 4:
                                s.wait_ge(eDVE.h, tok_oadd(k - 4))
                            r0 = r_base + td * 128
                            i = s.dma_start(out=x_st[k % 4][:],
                                            in_=x_in[r0:r0 + 128, hc * 128:(hc + 1) * 128])
                            xctoks[k] = qXc[k % 4].inc(i, 16)

                    @block.tensor
                    def _(t, hc=hc):
                        if hc == 0:
                            bar(t)
                        t.wait_ge(qW[hc % 2].h, wdtoks[hc])
                        if hc >= 2:
                            t.wait_ge(eDVE.h, DVE0 + (hc - 2) * dve_per + 1)
                        for pc in range(NPC):
                            i = t.matmul(pz[hc % 2][:, 0:TGD * 128],
                                         lhsT=wd_buf[hc % 2][:, pc * 128:(pc + 1) * 128],
                                         rhs=arena[:, pc * (TGD * 128):(pc + 1) * (TGD * 128)],
                                         start=(pc == 0), stop=(pc == NPC - 1))
                        ePE.inc(i)
                        t.wait_ge(eDVE.h, DVE0 + hc * dve_per + 1)
                        for td in range(TGD):
                            k = hc * TGD + td
                            if k >= 2:
                                t.wait_ge(eDVE.h, tok_oadd(k - 2))
                            i = t.transpose(ptz[k % 2][:, 0:128],
                                            in_=z_st[hc % 2][:, td * 128:(td + 1) * 128],
                                            identity=idf_sb[:])
                            ePE.inc(i)

                    @block.vector
                    def _(v, hc=hc):
                        if hc == 0:
                            bar(v)
                        v.wait_ge(ePE.h, PE0 + hc * pe_per + 1)
                        v.wait_ge(qGz[hc % 2].h, gztoks[hc])
                        i = v.tensor_tensor(out=z_st[hc % 2][:],
                                            in0=pz[hc % 2][:, 0:TGD * 128],
                                            in1=gz_st[hc % 2][:, 0:TGD * 128], op=ALU.mult)
                        eDVE.inc(i)
                        for td in range(TGD):
                            k = hc * TGD + td
                            v.wait_ge(ePE.h, PE0 + hc * pe_per + 1 + td + 1)
                            v.wait_ge(qXc[k % 4].h, xctoks[k])
                            if k >= 4:
                                v.wait_ge(qO[k % 4].h, otoks[k - 4])
                            i = v.tensor_tensor(out=o_st[k % 4][:],
                                                in0=ptz[k % 2][:, 0:128],
                                                in1=x_st[k % 4][:], op=ALU.add)
                            eDVE.inc(i)

                    @block.sync
                    def _(s, hc=hc):
                        for td in range(TGD):
                            k = hc * TGD + td
                            s.wait_ge(eDVE.h, tok_oadd(k))
                            r0 = r_base + td * 128
                            i = s.dma_start(out=out_d[r0:r0 + 128, hc * 128:(hc + 1) * 128],
                                            in_=o_st[k % 4][:])
                            otoks[k] = qO[k % 4].inc(i, 16)
                assert ePE.n == PE0 + HC * pe_per
                assert eDVE.n == DVE0 + HC * dve_per
            take_snap()

        def dump_xn(ga):
            @block.sync
            def _(s):
                bar(s)
                s.dma_start(out=xn_dump[ga], in_=arena[:, 0:HC * GA]).then_inc(qS.h, 16)
                qS.n += 16
            take_snap()

        init_phase()
        for ga in range(n_ga):
            phase_ln(ga)
            if dump:
                dump_xn(ga)
            phase_gu(ga)
            phase_ffn(ga)
            for gd in range(n_gd):
                phase_sel(ga, gd)
                phase_down(ga, gd)

        @block.sync
        def _(s):
            for sm in sems.values():
                if sm.n > 0:
                    s.wait_ge(sm.h, sm.n)

        @block.gpsimd
        def _(g):
            for sm in sems.values():
                if sm.n > 0:
                    g.wait_ge(sm.h, sm.n)

    return nc



def _prep_weights(inputs):
    w_gate = np.ascontiguousarray(np.asarray(inputs["L0_attn_w_gate"], dtype=np.float32))
    w_up = np.ascontiguousarray(np.asarray(inputs["L0_attn_w_up"], dtype=np.float32))
    w_down = np.ascontiguousarray(np.asarray(inputs["L0_attn_w_down"], dtype=np.float32))
    W_base = np.ascontiguousarray(np.asarray(inputs["L0_ffn_W_base"], dtype=np.float32))
    b_base = np.asarray(inputs["L0_ffn_b_base"], dtype=np.float32)
    W_router = np.asarray(inputs["L0_ffn_W_router"], dtype=np.float32)
    A = np.asarray(inputs["L0_ffn_A"], dtype=np.float32)
    B = np.asarray(inputs["L0_ffn_B"], dtype=np.float32)

    def gu_img(w):
        wp = np.zeros((IP, H), np.float32)
        wp[:I_TRUE] = w
        t = wp.reshape(NIC, WI, HC, 128)
        return np.ascontiguousarray(t.transpose(0, 3, 2, 1)).reshape(NIC, 128, HC * WI)

    wg_img = gu_img(w_gate)
    wu_img = gu_img(w_up)

    wdp = np.zeros((H, IP), np.float32)
    wdp[:, :I_TRUE] = w_down
    t = wdp.reshape(HC, 128, NPC, 128)
    wd_img = np.ascontiguousarray(t.transpose(0, 3, 2, 1)).reshape(HC, 128, NPC * 128)

    t = W_base.reshape(HC, 128, HC, 128)
    wb_img = np.ascontiguousarray(t.transpose(0, 3, 2, 1)).reshape(HC, 128, HC * 128)

    A2 = A.reshape(128, H)
    a_img = np.ascontiguousarray(A2.T.reshape(HC, 128, 128).transpose(1, 0, 2)
                                 ).reshape(128, HC * 128)

    b_img = np.ascontiguousarray(B.transpose(0, 2, 1)).reshape(128, H)

    wr_img = np.ascontiguousarray(W_router.T.reshape(HC, 128, 8).transpose(1, 0, 2)
                                  ).reshape(128, HC * 8)
    bb_img = np.ascontiguousarray(b_base.reshape(HC, 128).T)

    identb = np.eye(128, dtype=ml_dtypes.bfloat16)
    identf = np.eye(128, dtype=np.float32)
    eexp = np.zeros((8, 128), dtype=ml_dtypes.bfloat16)
    for e in range(8):
        eexp[e, e * 16:(e + 1) * 16] = 1

    return dict(wg=wg_img, wu=wu_img, wd=wd_img, wb=wb_img, a_img=a_img,
                b_img=b_img, wr=wr_img, bb=bb_img, identb=identb,
                identf=identf, eexp=eexp)


def kernel(**inputs):
    x = np.asarray(inputs["L0_attn_x"], dtype=np.float32).reshape(16384, H)
    shared = _prep_weights(inputs)
    T_core = 16384 // N_CORES
    nc = build_nc(T_core)
    in_maps = []
    for c in range(N_CORES):
        m = dict(shared)
        m["x"] = np.ascontiguousarray(x[c * T_core:(c + 1) * T_core])
        in_maps.append(m)
    res = run_bass_kernel_spmd(nc, in_maps, core_ids=list(range(N_CORES)))
    out = np.concatenate([np.asarray(r["out"]) for r in res.results], axis=0)
    return out.reshape(1, 16384, H).astype(np.float32)


# version 10
# speedup vs baseline: 1.1259x; 1.1259x over previous
import numpy as np
import ml_dtypes
import concourse.bass as bass
import concourse.mybir as mybir
from concourse.bass_utils import run_bass_kernel_spmd
from contextlib import ExitStack

F32 = mybir.dt.float32
BF16 = mybir.dt.bfloat16
AF = mybir.ActivationFunctionType
ALU = mybir.AluOpType
AX = mybir.AxisListType.X

N_CORES = 8
H = 4096
I_TRUE = 11008
IP = 11264
HC = 32
WI = 256
NIC = IP // WI
NPC = IP // 128
NSTAR = 5504.0
C0 = 0.5184071992260827
WARM = (1.0, 0.87, 1.15)
N_IT = 12
ACT_COUNT = True
EPS = 1e-5
CLAMP_D = 0.001


class Sem:
    def __init__(self, handle):
        self.h = handle
        self.n = 0

    def inc(self, instr, k=1):
        instr.then_inc(self.h, k)
        self.n += k
        return self.n


def build_nc(T_core, GA=1024, GD=512, dump=False):
    GA = min(GA, T_core)
    GD = min(GD, GA)
    assert GA % 512 == 0 and GD % 128 == 0
    NT = T_core // 128
    TGA = GA // 128
    TGD = GD // 128
    n_ga = T_core // GA
    n_gd = GA // GD
    n_tch = GA // 512

    nc = bass.Bass("TRN2", target_bir_lowering=False, debug=False)

    x_in = nc.dram_tensor("x", [T_core, H], F32, kind="ExternalInput").ap()
    wg_d = nc.dram_tensor("wg", [NIC, 128, HC * WI], F32, kind="ExternalInput").ap()
    wu_d = nc.dram_tensor("wu", [NIC, 128, HC * WI], F32, kind="ExternalInput").ap()
    wd_d = nc.dram_tensor("wd", [HC, 128, NPC * 128], F32, kind="ExternalInput").ap()
    wb_d = nc.dram_tensor("wb", [HC, 128, HC * 128], F32, kind="ExternalInput").ap()
    a_d = nc.dram_tensor("a_img", [128, HC * 128], F32, kind="ExternalInput").ap()
    b_d = nc.dram_tensor("b_img", [128, HC * 128], F32, kind="ExternalInput").ap()
    wr_d = nc.dram_tensor("wr", [128, HC * 8], F32, kind="ExternalInput").ap()
    bb_d = nc.dram_tensor("bb", [128, HC], F32, kind="ExternalInput").ap()
    idb_d = nc.dram_tensor("identb", [128, 128], BF16, kind="ExternalInput").ap()
    idf_d = nc.dram_tensor("identf", [128, 128], F32, kind="ExternalInput").ap()
    e_d = nc.dram_tensor("eexp", [8, 128], BF16, kind="ExternalInput").ap()

    out_d = nc.dram_tensor("out", [T_core, H], F32, kind="ExternalOutput").ap()

    scr_kind = "ExternalOutput" if dump else "Internal"
    a_scr = nc.dram_tensor("a_scr", [NT, 128, IP], F32, kind=scr_kind).ap()
    wdbf_scr = nc.dram_tensor("wdbf_scr", [HC, 128, NPC * 128], BF16, kind="Internal").ap()
    u_scr = nc.dram_tensor("u_scr", [NT, 128, IP], BF16, kind=scr_kind).ap()
    g_scr = nc.dram_tensor("g_scr", [n_ga, HC, 128, GA], BF16, kind=scr_kind).ap()
    tau_d = None
    xn_dump = None
    if dump:
        tau_d = nc.dram_tensor("tau_dump", [NT, 128, 1], F32, kind="ExternalOutput").ap()
        p_dump = nc.dram_tensor("p_dump", [n_ga * n_gd, 128, NPC * TGD * 128], BF16,
                                kind="ExternalOutput").ap()
        xn_dump = nc.dram_tensor("xn_dump", [n_ga, 128, HC * GA], BF16,
                                 kind="ExternalOutput").ap()

    ctx = ExitStack()
    with ctx:
        sb = lambda nm, sh, dt: ctx.enter_context(nc.sbuf_tensor(nm, sh, dt))

        AR_COLS = max(HC * GA, NPC * TGD * 128)
        arena = sb("arena", [128, AR_COLS], BF16)

        WS_COLS = 22016
        ws = sb("ws", [128, WS_COLS], F32)
        ws_bf = ws[:].bitcast(BF16)

        wg_buf = [ws_bf[:, 0:8192], ws_bf[:, 8192:16384]]
        wu_buf = [ws_bf[:, 16384:24576], ws_bf[:, 24576:32768]]
        x_sb = [ws[:, 0:H], ws[:, H:2 * H]]
        xn_tm = [ws_bf[:, 16384:16384 + H], ws_bf[:, 16384 + H:16384 + 2 * H]]
        junk_ln = ws_bf[:, 16384 + 2 * H:16384 + 3 * H]
        a_sb = ws[:, 0:IP]
        junk_dve = ws_bf[:, 22528:28160]
        junk_act = ws_bf[:, 28160:30976]
        ab_slot = ws_bf[:, 30976:35072]
        wb_buf = [ws_bf[:, 35072:39168], ws_bf[:, 39168:43264]]
        wd_buf = ws_bf[:, 30976:42240]
        u_prod = ws_bf[:, 22528:33792]

        Wr_sb = sb("Wr_sb", [128, HC * 8], BF16)
        bb_sb = sb("bb_sb", [128, HC], F32)
        idb_sb = sb("idb_sb", [128, 128], BF16)
        idf_sb = sb("idf_sb", [128, 128], F32)
        e_sb = sb("e_sb", [8, 128], BF16)
        eps_sb = sb("eps_sb", [128, 1], F32)

        NS = 2
        sgn_st = [sb(f"sgn{j}", [128, WI], BF16) for j in range(NS)]
        s_st = [sb(f"s{j}", [128, WI], F32) for j in range(NS)]
        aa_st = [sb(f"aa{j}", [128, WI], F32) for j in range(NS)]
        uu_st = [sb(f"uu{j}", [128, WI], BF16) for j in range(NS)]
        m1_parts = sb("m1p", [128, NT * NIC], F32)
        g_st = [sb(f"g{j}", [128, 512], BF16) for j in range(2)]
        gz_st = [sb(f"gz{j}", [128, 512], BF16) for j in range(2)]
        z_st = [sb(f"z{j}", [128, 512], F32) for j in range(2)]
        x_st = [sb(f"xst{j}", [128, 128], F32) for j in range(4)]
        o_st = [sb(f"ost{j}", [128, 128], F32) for j in range(4)]
        p_st5 = [sb(f"pst{j}", [128, 512], BF16) for j in range(2)]
        w_tm = sb("w_tm", [128, 128], F32)
        lg_sb = sb("lg", [128, 8], F32)
        m8_sb = sb("m8", [128, 8], F32)
        eq_sb = sb("eq", [128, 8], F32)
        wt2_sb = sb("wt2", [128, 8], F32)
        w_fm = sb("w_fm", [8, GA], BF16)
        w_exp = sb("w_exp", [128, GA], BF16)
        hw_sb = sb("hw", [128, GA], BF16)
        st_ = {}
        for nm in ["sumx", "mean", "sumsq", "mean2", "var", "std", "rstd",
                   "negmr", "m1", "tau0", "lo", "hi", "nlo", "nhi", "mid",
                   "cnt", "pred", "npred", "t0", "t1", "u0", "d_", "dn",
                   "rr", "f0", "e0", "d21", "e2", "den", "rw1", "rw2"]:
            st_[nm] = sb("sc_" + nm, [128, 1], F32)
        pred_u8 = sb("pred_u8", [128, 1], mybir.dt.uint8)
        tau_st = sb("tau_st", [128, TGD], F32)

        base_names = ["qG", "qS", "ePE", "eACT", "eDVE", "qSEL", "qD", "qAB", "qWC", "qUP"]
        slot_specs = [("qW", 2), ("qX", 2), ("qA", 3), ("qU", 3),
                      ("qGs", 2), ("qGz", 2), ("qXc", 4), ("qO", 4), ("qWD", 4)]
        sems = {nm: Sem(ctx.enter_context(nc.semaphore(nm))) for nm in base_names}
        for nm, k in slot_specs:
            for j in range(k):
                sems[f"{nm}{j}"] = Sem(ctx.enter_context(nc.semaphore(f"{nm}{j}")))
        qG, qS, ePE, eACT, eDVE = (sems[k] for k in ["qG", "qS", "ePE", "eACT", "eDVE"])
        qSEL, qD = sems["qSEL"], sems["qD"]
        qW = [sems[f"qW{j}"] for j in range(2)]
        qX = [sems[f"qX{j}"] for j in range(2)]
        qA = [sems[f"qA{j}"] for j in range(3)]
        qU = [sems[f"qU{j}"] for j in range(3)]
        qGs = [sems[f"qGs{j}"] for j in range(2)]
        qGz = [sems[f"qGz{j}"] for j in range(2)]
        qXc = [sems[f"qXc{j}"] for j in range(4)]
        qO = [sems[f"qO{j}"] for j in range(4)]
        qWD = [sems[f"qWD{j}"] for j in range(4)]
        qAB, qWC = sems["qAB"], sems["qWC"]

        block = ctx.enter_context(nc.Block())

        snap = {k: 0 for k in sems}

        def take_snap():
            for k in sems:
                snap[k] = sems[k].n

        def bar(eng):
            for nm in sems:
                if snap[nm] > 0:
                    eng.wait_ge(sems[nm].h, snap[nm])

        def settle(eng, instr):
            eDVE.inc(instr)
            eng.wait_ge(eDVE.h, eDVE.n)

        def init_phase():
            @block.gpsimd
            def _(g):
                g.dma_start(out=Wr_sb[:], in_=wr_d).then_inc(qG.h, 16)
                qG.n += 16

            @block.sync
            def _(s):
                s.dma_start(out=bb_sb[:], in_=bb_d).then_inc(qS.h, 16)
                s.dma_start(out=idb_sb[:], in_=idb_d).then_inc(qS.h, 16)
                s.dma_start(out=idf_sb[:], in_=idf_d).then_inc(qS.h, 16)
                s.dma_start(out=e_sb[:], in_=e_d).then_inc(qS.h, 16)
                qS.n += 64

            @block.vector
            def _(v):
                v.memset(eps_sb[:], EPS)
                i = v.memset(w_tm[:], 0.0)
                eDVE.inc(i)
            take_snap()

        def phase_ln(ga):
            PE0, ACT0, DVE0 = ePE.n, eACT.n, eDVE.n
            xtoks = [0] * TGA
            with ExitStack() as pctx:
                ptp = [pctx.enter_context(nc.psum_tensor(f"pln{ga}_{j}", [128, 1024], BF16))
                       for j in range(2)]
                for tt in range(TGA):
                    t_abs = ga * TGA + tt
                    r0 = t_abs * 128

                    @block.sync
                    def _(s, tt=tt, r0=r0):
                        if tt == 0:
                            bar(s)
                        if tt >= 2:
                            s.wait_ge(eACT.h, ACT0 + 2 * (tt - 2) + 2)
                        i = s.dma_start(out=x_sb[tt % 2], in_=x_in[r0:r0 + 128, :])
                        xtoks[tt] = qX[tt % 2].inc(i, 16)

                    @block.vector
                    def _(v, tt=tt):
                        if tt == 0:
                            bar(v)
                        v.wait_ge(qX[tt % 2].h, xtoks[tt])
                        xs = x_sb[tt % 2]
                        settle(v, v.reduce_sum(st_["sumx"][:], xs, axis=AX))
                        settle(v, v.tensor_scalar(out=st_["mean"][:], in0=st_["sumx"][:],
                                                  scalar1=1.0 / H, scalar2=None,
                                                  op0=ALU.mult))
                        settle(v, v.scalar_tensor_tensor(out=junk_ln, in0=xs, scalar=1.0,
                                                         in1=xs, op0=ALU.mult,
                                                         op1=ALU.mult,
                                                         accum_out=st_["sumsq"][:]))
                        settle(v, v.tensor_tensor(out=st_["mean2"][:], in0=st_["mean"][:],
                                                  in1=st_["mean"][:], op=ALU.mult))
                        settle(v, v.tensor_scalar(out=st_["var"][:], in0=st_["sumsq"][:],
                                                  scalar1=1.0 / H, scalar2=None,
                                                  op0=ALU.mult))
                        settle(v, v.tensor_tensor(out=st_["var"][:], in0=st_["var"][:],
                                                  in1=st_["mean2"][:], op=ALU.subtract))
                        i = v.tensor_copy(out=st_["sumx"][:], in_=st_["var"][:])
                        eDVE.inc(i)

                    @block.scalar
                    def _(s, tt=tt):
                        if tt == 0:
                            bar(s)
                        s.wait_ge(eDVE.h, DVE0 + 43 * tt + 7)
                        s.activation(out=st_["std"][:], in_=st_["var"][:],
                                     func=AF.Sqrt, bias=eps_sb[:])
                        i = s.activation(out=st_["u0"][:], in_=st_["std"][:],
                                         func=AF.Identity)
                        eACT.inc(i)

                    @block.vector
                    def _(v, tt=tt):
                        v.wait_ge(eACT.h, ACT0 + 2 * tt + 1)
                        settle(v, v.reciprocal(st_["rstd"][:], st_["std"][:]))
                        settle(v, v.tensor_tensor(out=st_["negmr"][:], in0=st_["mean"][:],
                                                  in1=st_["rstd"][:], op=ALU.mult))
                        settle(v, v.tensor_scalar(out=st_["negmr"][:], in0=st_["negmr"][:],
                                                  scalar1=-1.0, scalar2=None, op0=ALU.mult))
                        i = v.tensor_copy(out=st_["mean2"][:], in_=st_["negmr"][:])
                        eDVE.inc(i)

                    @block.scalar
                    def _(s, tt=tt):
                        s.wait_ge(eDVE.h, DVE0 + 43 * tt + 11)
                        i = s.activation(out=xn_tm[tt % 2], in_=x_sb[tt % 2],
                                         func=AF.Identity, bias=st_["negmr"][:],
                                         scale=st_["rstd"][:])
                        eACT.inc(i)

                    @block.tensor
                    def _(t, tt=tt):
                        if tt == 0:
                            bar(t)
                        t.wait_ge(eACT.h, ACT0 + 2 * tt + 2)
                        for hb in range(HC):
                            k = tt * HC + hb
                            if k >= 2:
                                kc = k - 2
                                t.wait_ge(eDVE.h,
                                          DVE0 + 43 * (kc // HC) + 12 + (kc % HC))
                            i = t.transpose(ptp[k % 2][:, 0:128],
                                            in_=xn_tm[tt % 2][:, hb * 128:(hb + 1) * 128],
                                            identity=idb_sb[:])
                            ePE.inc(i)

                    @block.vector
                    def _(v, tt=tt):
                        for hb in range(HC):
                            k = tt * HC + hb
                            v.wait_ge(ePE.h, PE0 + k + 1)
                            col = hb * GA + tt * 128
                            i = v.tensor_copy(out=arena[:, col:col + 128],
                                              in_=ptp[k % 2][:, 0:128])
                            eDVE.inc(i)
                assert eDVE.n == DVE0 + 43 * TGA
                assert eACT.n == ACT0 + 2 * TGA
                assert ePE.n == PE0 + HC * TGA
            take_snap()

        def phase_gu(ga):
            PE0, ACT0, DVE0 = ePE.n, eACT.n, eDVE.n
            wtoks = [0] * NIC
            atoks = [0] * (NIC * TGA)
            utoks = [0] * (NIC * TGA)
            qA0 = [qA[j].n for j in range(NS)]
            qU0 = [qU[j].n for j in range(NS)]

            def atok(k):
                return qA0[k % NS] + 16 * (k // NS + 1)

            def utok(k):
                return qU0[k % NS] + 16 * (k // NS + 1)
            with ExitStack() as pctx:
                pgu = [pctx.enter_context(nc.psum_tensor(f"pgu{ga}_{t}", [128, 512], F32))
                       for t in range(TGA)]
                for ic in range(NIC):
                    @block.gpsimd
                    def _(g, ic=ic):
                        if ic == 0:
                            bar(g)
                        if ic >= 2:
                            g.wait_ge(ePE.h, PE0 + (ic - 1) * TGA)
                        qW[ic % 2].inc(g.dma_start(out=wg_buf[ic % 2], in_=wg_d[ic]), 16)
                        i = g.dma_start(out=wu_buf[ic % 2], in_=wu_d[ic])
                        wtoks[ic] = qW[ic % 2].inc(i, 16)

                    @block.tensor
                    def _(t, ic=ic):
                        if ic == 0:
                            bar(t)
                        t.wait_ge(qW[ic % 2].h, wtoks[ic])
                        for tt in range(TGA):
                            if ic >= 1:
                                kprev = (ic - 1) * TGA + tt
                                t.wait_ge(eACT.h, ACT0 + 2 * kprev + 1)
                                t.wait_ge(eDVE.h, DVE0 + kprev + 1)
                            for hc in range(HC):
                                t.matmul(pgu[tt][:, 0:WI],
                                         lhsT=arena[:, hc * GA + tt * 128:hc * GA + tt * 128 + 128],
                                         rhs=wg_buf[ic % 2][:, hc * WI:(hc + 1) * WI],
                                         start=(hc == 0), stop=(hc == HC - 1))
                            for hc in range(HC):
                                i = t.matmul(pgu[tt][:, WI:2 * WI],
                                             lhsT=arena[:, hc * GA + tt * 128:hc * GA + tt * 128 + 128],
                                             rhs=wu_buf[ic % 2][:, hc * WI:(hc + 1) * WI],
                                             start=(hc == 0), stop=(hc == HC - 1))
                            ePE.inc(i)

                    @block.scalar
                    def _(s, ic=ic):
                        if ic == 0:
                            bar(s)
                        for tt in range(TGA):
                            k = ic * TGA + tt
                            t_abs = ga * TGA + tt
                            s.wait_ge(ePE.h, PE0 + k + 1)
                            if k >= NS:
                                s.wait_ge(eDVE.h, DVE0 + (k - NS) + 1)
                                s.wait_ge(qA[(k - NS) % NS].h, atok(k - NS))
                            sl = k % NS
                            s.activation(out=sgn_st[sl][:], in_=pgu[tt][:, 0:WI],
                                         func=AF.Sign)
                            i = s.activation(out=s_st[sl][:], in_=pgu[tt][:, 0:WI],
                                             func=AF.Silu)
                            eACT.inc(i)
                            i = s.activation(out=aa_st[sl][:], in_=s_st[sl][:],
                                             func=AF.Abs,
                                             accum_out=m1_parts[:, t_abs * NIC + ic:
                                                               t_abs * NIC + ic + 1])
                            eACT.inc(i)

                    @block.vector
                    def _(v, ic=ic):
                        if ic == 0:
                            bar(v)
                        for tt in range(TGA):
                            k = ic * TGA + tt
                            sl = k % NS
                            v.wait_ge(eACT.h, ACT0 + 2 * k + 1)
                            if k >= NS:
                                v.wait_ge(qU[(k - NS) % NS].h, utok(k - NS))
                            i = v.tensor_tensor(out=uu_st[sl][:],
                                                in0=pgu[tt][:, WI:2 * WI],
                                                in1=sgn_st[sl][:], op=ALU.mult)
                            eDVE.inc(i)

                    @block.sync
                    def _(s, ic=ic):
                        if ic == 0:
                            bar(s)
                        for tt in range(TGA):
                            k = ic * TGA + tt
                            t_abs = ga * TGA + tt
                            sl = k % NS
                            s.wait_ge(eACT.h, ACT0 + 2 * k + 2)
                            i = s.dma_start(out=a_scr[t_abs, :, ic * WI:(ic + 1) * WI],
                                            in_=aa_st[sl][:])
                            atoks[k] = qA[k % NS].inc(i, 16)
                            assert atoks[k] == atok(k)
                            s.wait_ge(eDVE.h, DVE0 + k + 1)
                            i = s.dma_start(out=u_scr[t_abs, :, ic * WI:(ic + 1) * WI],
                                            in_=uu_st[sl][:])
                            utoks[k] = qU[k % NS].inc(i, 16)
                            assert utoks[k] == utok(k)
                assert ePE.n == PE0 + NIC * TGA
                assert eACT.n == ACT0 + 2 * NIC * TGA
                assert eDVE.n == DVE0 + NIC * TGA
            take_snap()

        def phase_ffn(ga):
            PE0, ACT0, DVE0 = ePE.n, eACT.n, eDVE.n
            wbtoks = [0] * HC
            gstoks = [0] * (HC * n_tch)
            with ExitStack() as pctx:
                pb = [pctx.enter_context(nc.psum_tensor(f"pb{ga}_{j}", [128, 512], F32))
                      for j in range(2)]
                phh = pctx.enter_context(nc.psum_tensor(f"phh{ga}", [128, 512], F32))
                pr = pctx.enter_context(nc.psum_tensor(f"pr{ga}", [128, 512], F32))
                pwt = pctx.enter_context(nc.psum_tensor(f"pwt{ga}", [128, 512], F32))
                pex = pctx.enter_context(nc.psum_tensor(f"pex{ga}", [128, 512], F32))

                ab_toks = [0, 0]

                @block.gpsimd
                def _(g):
                    bar(g)
                    i = g.dma_start(out=ab_slot, in_=a_d)
                    ab_toks[0] = qAB.inc(i, 16)

                for tt in range(TGA):
                    @block.tensor
                    def _(t, tt=tt):
                        if tt == 0:
                            bar(t)
                        else:
                            t.wait_ge(eDVE.h, DVE0 + 13 * tt)
                        for hc in range(HC):
                            i = t.matmul(pr[:, 0:8],
                                         lhsT=arena[:, hc * GA + tt * 128:hc * GA + tt * 128 + 128],
                                         rhs=Wr_sb[:, hc * 8:(hc + 1) * 8],
                                         start=(hc == 0), stop=(hc == HC - 1))
                        ePE.inc(i)

                    @block.vector
                    def _(v, tt=tt):
                        if tt == 0:
                            bar(v)
                        v.wait_ge(ePE.h, PE0 + 2 * tt + 1)
                        settle(v, v.tensor_copy(out=lg_sb[:], in_=pr[:, 0:8]))
                        settle(v, v.max(out=m8_sb[:], in_=lg_sb[:]))
                        settle(v, v.tensor_tensor(out=st_["d21"][:], in0=m8_sb[:, 1:2],
                                                  in1=m8_sb[:, 0:1], op=ALU.subtract))
                        i = v.tensor_copy(out=st_["t0"][:], in_=st_["d21"][:])
                        eDVE.inc(i)

                    @block.scalar
                    def _(s, tt=tt):
                        if tt == 0:
                            bar(s)
                        s.wait_ge(eDVE.h, DVE0 + 13 * tt + 4)
                        s.activation(out=st_["e2"][:], in_=st_["d21"][:], func=AF.Exp)
                        i = s.activation(out=st_["u0"][:], in_=st_["e2"][:],
                                         func=AF.Identity)
                        eACT.inc(i)

                    @block.vector
                    def _(v, tt=tt):
                        v.wait_ge(eACT.h, ACT0 + tt + 1)
                        if tt >= 1:
                            v.wait_ge(ePE.h, PE0 + 2 * tt)
                        settle(v, v.tensor_scalar(out=st_["den"][:], in0=st_["e2"][:],
                                                  scalar1=1.0, scalar2=None, op0=ALU.add))
                        settle(v, v.reciprocal(st_["rw1"][:], st_["den"][:]))
                        settle(v, v.tensor_scalar(out=st_["rw2"][:], in0=st_["rw1"][:],
                                                  scalar1=-1.0, scalar2=1.0,
                                                  op0=ALU.mult, op1=ALU.add))
                        settle(v, v.tensor_tensor(out=eq_sb[:], in0=lg_sb[:],
                                                  in1=m8_sb[:, 0:1].to_broadcast([128, 8]),
                                                  op=ALU.is_equal))
                        settle(v, v.tensor_tensor(out=wt2_sb[:], in0=eq_sb[:],
                                                  in1=st_["rw1"][:].to_broadcast([128, 8]),
                                                  op=ALU.mult))
                        settle(v, v.tensor_tensor(out=eq_sb[:], in0=lg_sb[:],
                                                  in1=m8_sb[:, 1:2].to_broadcast([128, 8]),
                                                  op=ALU.is_equal))
                        settle(v, v.tensor_tensor(out=eq_sb[:], in0=eq_sb[:],
                                                  in1=st_["rw2"][:].to_broadcast([128, 8]),
                                                  op=ALU.mult))
                        i = v.tensor_tensor(out=w_tm[:, 0:8], in0=eq_sb[:],
                                            in1=wt2_sb[:], op=ALU.add)
                        eDVE.inc(i)

                    @block.tensor
                    def _(t, tt=tt):
                        t.wait_ge(eDVE.h, DVE0 + 13 * tt + 12)
                        i = t.transpose(pwt[:, 0:128], in_=w_tm[:], identity=idf_sb[:])
                        ePE.inc(i)

                    @block.vector
                    def _(v, tt=tt):
                        v.wait_ge(ePE.h, PE0 + 2 * tt + 2)
                        i = v.tensor_copy(out=w_fm[:, tt * 128:(tt + 1) * 128],
                                          in_=pwt[0:8, 0:128])
                        eDVE.inc(i)

                for tch in range(n_tch):
                    @block.tensor
                    def _(t, tch=tch):
                        t.wait_ge(eDVE.h, DVE0 + 13 * (4 * tch + 4))
                        if tch == 0:
                            t.wait_ge(qAB.h, ab_toks[0])
                        if tch >= 1:
                            t.wait_ge(eDVE.h, DVE0 + 13 * TGA + 2 * tch)
                        i = t.matmul(pex[:, 0:512], lhsT=e_sb[:],
                                     rhs=w_fm[:, tch * 512:(tch + 1) * 512],
                                     start=True, stop=True)
                        ePE.inc(i)
                        for hc in range(HC):
                            i = t.matmul(phh[:, 0:512],
                                         lhsT=ab_slot[:, hc * 128:(hc + 1) * 128],
                                         rhs=arena[:, hc * GA + tch * 512:
                                                   hc * GA + (tch + 1) * 512],
                                         start=(hc == 0), stop=(hc == HC - 1))
                        ePE.inc(i)

                    @block.vector
                    def _(v, tch=tch):
                        v.wait_ge(ePE.h, PE0 + 2 * TGA + 2 * tch + 1)
                        i = v.tensor_copy(out=w_exp[:, tch * 512:(tch + 1) * 512],
                                          in_=pex[:, 0:512])
                        eDVE.inc(i)
                        v.wait_ge(ePE.h, PE0 + 2 * TGA + 2 * tch + 2)
                        i = v.scalar_tensor_tensor(out=hw_sb[:, tch * 512:(tch + 1) * 512],
                                                   in0=phh[:, 0:512], scalar=2.0,
                                                   in1=w_exp[:, tch * 512:(tch + 1) * 512],
                                                   op0=ALU.mult, op1=ALU.mult)
                        eDVE.inc(i)

                DVEH = eDVE.n
                PEH = PE0 + 2 * TGA + 2 * n_tch

                for oc in range(HC):
                    @block.gpsimd
                    def _(g, oc=oc):
                        if oc == 0:
                            g.wait_ge(ePE.h, PEH)
                            i = g.dma_start(out=ab_slot, in_=b_d)
                            ab_toks[1] = qAB.inc(i, 16)
                        if oc >= 2:
                            g.wait_ge(ePE.h, PEH + (oc - 1) * n_tch)
                        i = g.dma_start(out=wb_buf[oc % 2], in_=wb_d[oc])
                        wbtoks[oc] = qW[oc % 2].inc(i, 16)

                    @block.tensor
                    def _(t, oc=oc):
                        t.wait_ge(qW[oc % 2].h, wbtoks[oc])
                        if oc == 0:
                            t.wait_ge(eDVE.h, DVEH)
                            t.wait_ge(qAB.h, ab_toks[1])
                        for tch in range(n_tch):
                            j = oc * n_tch + tch
                            if j >= 2:
                                t.wait_ge(eACT.h, ACT0 + TGA + (j - 1))
                            for hc in range(HC):
                                t.matmul(pb[j % 2][:, 0:512],
                                         lhsT=wb_buf[oc % 2][:, hc * 128:(hc + 1) * 128],
                                         rhs=arena[:, hc * GA + tch * 512:
                                                   hc * GA + (tch + 1) * 512],
                                         start=(hc == 0), stop=False)
                            i = t.matmul(pb[j % 2][:, 0:512],
                                         lhsT=ab_slot[:, oc * 128:(oc + 1) * 128],
                                         rhs=hw_sb[:, tch * 512:(tch + 1) * 512],
                                         start=False, stop=True)
                            ePE.inc(i)

                    @block.scalar
                    def _(s, oc=oc):
                        for tch in range(n_tch):
                            j = oc * n_tch + tch
                            s.wait_ge(ePE.h, PEH + j + 1)
                            if j >= 2:
                                s.wait_ge(qGs[j % 2].h, gstoks[j - 2])
                            i = s.activation(out=g_st[j % 2][:], in_=pb[j % 2][:, 0:512],
                                             func=AF.Silu, bias=bb_sb[:, oc:oc + 1])
                            eACT.inc(i)

                    @block.sync
                    def _(s, oc=oc):
                        if oc == 0:
                            bar(s)
                        for tch in range(n_tch):
                            j = oc * n_tch + tch
                            s.wait_ge(eACT.h, ACT0 + TGA + j + 1)
                            i = s.dma_start(out=g_scr[ga, oc, :, tch * 512:(tch + 1) * 512],
                                            in_=g_st[j % 2][:])
                            gstoks[j] = qGs[j % 2].inc(i, 16)
                assert ePE.n == PEH + HC * n_tch
                assert eACT.n == ACT0 + TGA + HC * n_tch
                assert eDVE.n == DVE0 + 13 * TGA + 2 * n_tch
            take_snap()

        class ItA:

            def __init__(self, ga, gd, prev_a_free_tok):
                self.tiles = [ga * TGA + gd * TGD + td for td in range(TGD)]
                self.gd = gd
                self.prev_a_free = prev_a_free_tok
                self.atoks = [0] * TGD
                self.aloaded = 0
                self.td = 0
                self.it = -1
                self.g = 0
                self.mid_toks = []
                self.cnt_last = [0] * TGD
                self.end_tok = None
                self.act_start = eACT.n
                self.done_emitting = False

            def can_aload(self):
                if self.aloaded >= TGD:
                    return False
                if self.aloaded == 0:
                    return True
                return self.cnt_last[self.aloaded - 1] != 0

            def maybe_aload(self, s):
                while self.can_aload():
                    td = self.aloaded
                    gate = (self.prev_a_free if td == 0
                            else self.cnt_last[td - 1])
                    s.wait_ge(eDVE.h, gate)
                    i = s.dma_start(out=a_sb, in_=a_scr[self.tiles[td]])
                    self.atoks[td] = qSEL.inc(i, 16)
                    self.aloaded += 1
                    if td == 0:
                        break
                    break

            def emit_iter(self, v):
                if self.done_emitting:
                    return False
                S = lambda i: settle(v, i)
                ts, tt_, rc, sel = (v.tensor_scalar, v.tensor_tensor,
                                    v.reciprocal, v.select)
                if self.it < 0:
                    td = self.td
                    t_abs = self.tiles[td]
                    if self.atoks[td] == 0:
                        return True
                    v.wait_ge(qSEL.h, self.atoks[td])
                    v.memset(st_["lo"][:], 0.0)
                    v.memset(st_["nlo"][:], float(I_TRUE))
                    S(v.memset(st_["nhi"][:], 0.0))
                    S(v.reduce_sum(st_["m1"][:],
                                   m1_parts[:, t_abs * NIC:(t_abs + 1) * NIC],
                                   axis=AX))
                    S(ts(out=st_["tau0"][:], in0=st_["m1"][:],
                         scalar1=C0 / I_TRUE, scalar2=None, op0=ALU.mult))
                    S(ts(out=st_["hi"][:], in0=st_["tau0"][:],
                         scalar1=16.0, scalar2=None, op0=ALU.mult))
                    self.it = 0
                    return True
                it = self.it
                ill = it >= len(WARM)
                if it < len(WARM):
                    ts(out=st_["mid"][:], in0=st_["tau0"][:],
                       scalar1=WARM[it], scalar2=None, op0=ALU.mult)
                    S(ts(out=st_["t1"][:], in0=st_["hi"][:],
                         scalar1=0.99999, scalar2=None, op0=ALU.mult))
                else:
                    tt_(out=st_["d_"][:], in0=st_["hi"][:],
                        in1=st_["lo"][:], op=ALU.subtract)
                    tt_(out=st_["dn"][:], in0=st_["nlo"][:],
                        in1=st_["nhi"][:], op=ALU.subtract)
                    ts(out=st_["f0"][:], in0=st_["nlo"][:],
                       scalar1=-NSTAR, scalar2=None, op0=ALU.add)
                    S(ts(out=st_["t1"][:], in0=st_["hi"][:],
                         scalar1=0.99999, scalar2=None, op0=ALU.mult))
                    S(rc(st_["rr"][:], st_["dn"][:]))
                    S(tt_(out=st_["f0"][:], in0=st_["f0"][:],
                          in1=st_["d_"][:], op=ALU.mult))
                    S(tt_(out=st_["f0"][:], in0=st_["f0"][:],
                          in1=st_["rr"][:], op=ALU.mult))
                    S(tt_(out=st_["mid"][:], in0=st_["lo"][:],
                          in1=st_["f0"][:], op=ALU.add))
                i = tt_(out=st_["mid"][:], in0=st_["mid"][:],
                        in1=st_["t1"][:], op=ALU.min)
                eDVE.inc(i)
                v.wait_ge(eDVE.h, eDVE.n)
                i = v.tensor_copy(out=st_["t0"][:], in_=st_["mid"][:])
                eDVE.inc(i)
                self.mid_toks.append(eDVE.n)
                i = ts(out=junk_dve, in0=a_sb[:, 0:5632],
                       scalar1=st_["mid"][:], scalar2=0.0,
                       op0=ALU.is_gt, op1=ALU.add, accum_out=st_["cnt"][:])
                eDVE.inc(i)
                if it == N_IT - 1:
                    self.cnt_last[self.td] = eDVE.n
                v.wait_ge(eDVE.h, eDVE.n)
                v.wait_ge(eACT.h, self.act_start + self.g + 1)
                S(tt_(out=st_["t0"][:], in0=st_["d21"][:],
                      in1=st_["e2"][:], op=ALU.add))
                S(ts(out=st_["t0"][:], in0=st_["t0"][:],
                     scalar1=-0.5, scalar2=2816.0, op0=ALU.mult, op1=ALU.add))
                S(tt_(out=st_["cnt"][:], in0=st_["cnt"][:],
                      in1=st_["t0"][:], op=ALU.add))
                S(ts(out=pred_u8[:], in0=st_["cnt"][:],
                     scalar1=NSTAR, scalar2=None, op0=ALU.is_gt))
                if ill:
                    ts(out=st_["u0"][:], in0=st_["nlo"][:],
                       scalar1=0.5, scalar2=NSTAR * 0.5,
                       op0=ALU.mult, op1=ALU.add)
                    i = ts(out=st_["t1"][:], in0=st_["nhi"][:],
                           scalar1=0.5, scalar2=NSTAR * 0.5,
                           op0=ALU.mult, op1=ALU.add)
                else:
                    v.tensor_copy(out=st_["u0"][:], in_=st_["nlo"][:])
                    i = v.tensor_copy(out=st_["t1"][:], in_=st_["nhi"][:])
                eDVE.inc(i)
                v.wait_ge(eDVE.h, eDVE.n)
                sel(out=st_["lo"][:], mask=pred_u8[:],
                    on_true=st_["mid"][:], on_false=st_["lo"][:])
                sel(out=st_["hi"][:], mask=pred_u8[:],
                    on_true=st_["hi"][:], on_false=st_["mid"][:])
                sel(out=st_["nlo"][:], mask=pred_u8[:],
                    on_true=st_["cnt"][:], on_false=st_["u0"][:])
                i = sel(out=st_["nhi"][:], mask=pred_u8[:],
                        on_true=st_["t1"][:], on_false=st_["cnt"][:])
                eDVE.inc(i)
                v.wait_ge(eDVE.h, eDVE.n)
                self.g += 1
                self.it += 1
                if self.it == N_IT:
                    i = v.tensor_copy(out=tau_st[:, self.td:self.td + 1],
                                      in_=st_["hi"][:])
                    eDVE.inc(i)
                    v.wait_ge(eDVE.h, eDVE.n)
                    self.td += 1
                    self.it = -1
                    if self.td == TGD:
                        self.end_tok = eDVE.n
                        self.done_emitting = True
                        return False
                return True

            def drain(self):
                if not self.done_emitting:
                    @block.sync
                    def _(s):
                        self.maybe_aload(s)

                    @block.vector
                    def _(v):
                        while self.emit_iter(v):
                            if not self.done_emitting and self.it == -1 \
                                    and self.atoks[self.td] == 0:
                                break
                    if not self.done_emitting:
                        self.drain()

            def emit_act(self):
                assert eACT.n == self.act_start

                @block.scalar
                def _(s):
                    for g in range(TGD * N_IT):
                        s.wait_ge(eDVE.h, self.mid_toks[g])
                        s.activation(out=junk_act, in_=a_sb[:, 5632:8448],
                                     func=AF.Sign, bias=st_["mid"][:], scale=-1.0,
                                     accum_out=st_["d21"][:])
                        s.activation(out=junk_act, in_=a_sb[:, 8448:11264],
                                     func=AF.Sign, bias=st_["mid"][:], scale=-1.0,
                                     accum_out=st_["e2"][:])
                        i = s.activation(out=st_["den"][:], in_=st_["e2"][:],
                                         func=AF.Identity)
                        eACT.inc(i)

        def phase_sel_it(ga, gd, prev_a_free_tok):
            ita = ItA(ga, gd, prev_a_free_tok)

            @block.sync
            def _(s):
                bar(s)
                ita.maybe_aload(s)

            @block.vector
            def _(v):
                bar(v)
                ita.emit_iter(v)
                while not ita.done_emitting and (ita.it != -1 or ita.atoks[ita.td] != 0):
                    if not ita.emit_iter(v):
                        break
            ita.drain()
            ita.emit_act()
            take_snap()
            return ita

        def phase_products(ga, gd, a_free_tok):
            prev_tok = a_free_tok
            for td in range(TGD):
                t_abs = ga * TGA + gd * TGD + td
                PE0, ACTP = ePE.n, eACT.n
                ld_toks = [0, 0]

                @block.sync
                def _(s, td=td, prev_tok=prev_tok):
                    if td == 0:
                        bar(s)
                    s.wait_ge(eDVE.h, prev_tok)
                    i = s.dma_start(out=a_sb, in_=a_scr[t_abs])
                    ld_toks[0] = qSEL.inc(i, 16)
                    i = s.dma_start(out=u_prod, in_=u_scr[t_abs])
                    ld_toks[1] = sems["qUP"].inc(i, 16)

                if dump:
                    @block.sync
                    def _(s, td=td):
                        i = s.dma_start(out=tau_d[t_abs],
                                        in_=tau_st[:, td:td + 1])
                        qD.inc(i, 16)

                with ExitStack() as pctx:
                    ptpb = [pctx.enter_context(
                        nc.psum_tensor(f"ptpb{t_abs}_{j}", [128, 1024], BF16))
                        for j in range(2)]
                    NCH = NPC // 4
                    tt512 = [0] * NCH

                    @block.vector
                    def _(v, td=td):
                        if td == 0:
                            bar(v)
                        v.wait_ge(qSEL.h, ld_toks[0])
                        v.wait_ge(sems["qUP"].h, ld_toks[1])
                        for c in range(NCH):
                            if c >= 2:
                                v.wait_ge(ePE.h, PE0 + 4 * (c - 1))
                            v.scalar_tensor_tensor(out=z_st[c % 2][:],
                                                   in0=a_sb[:, c * 512:(c + 1) * 512],
                                                   scalar=tau_st[:, td:td + 1],
                                                   in1=a_sb[:, c * 512:(c + 1) * 512],
                                                   op0=ALU.is_gt, op1=ALU.mult)
                            i = v.tensor_tensor(out=p_st5[c % 2][:],
                                                in0=z_st[c % 2][:],
                                                in1=u_prod[:, c * 512:(c + 1) * 512],
                                                op=ALU.mult)
                            tt512[c] = eDVE.inc(i)

                    prev_tok = eDVE.n

                    @block.tensor
                    def _(t, td=td):
                        if td == 0:
                            bar(t)
                        for c in range(NCH):
                            t.wait_ge(eDVE.h, tt512[c])
                            for q in range(4):
                                k = 4 * c + q
                                if k >= 2:
                                    t.wait_ge(eACT.h, ACTP + k - 1)
                                i = t.transpose(ptpb[k % 2][:, 0:128],
                                                in_=p_st5[c % 2][:, q * 128:(q + 1) * 128],
                                                identity=idb_sb[:])
                                ePE.inc(i)

                    @block.scalar
                    def _(s, td=td):
                        if td == 0:
                            bar(s)
                        for k in range(NPC):
                            s.wait_ge(ePE.h, PE0 + k + 1)
                            col = k * (TGD * 128) + td * 128
                            i = s.activation(out=arena[:, col:col + 128],
                                             in_=ptpb[k % 2][:, 0:128],
                                             func=AF.Identity)
                            eACT.inc(i)
            take_snap()
            return prev_tok

        def phase_down(ga, gd, ita=None):
            PE0 = ePE.n
            r_base = (ga * TGA + gd * TGD) * 128
            pe_per = 4 + TGD
            wdq = [[0] * 4 for _ in range(HC)]
            gztoks = [0] * HC
            xctoks = [0] * (HC * TGD)
            otoks = [0] * (HC * TGD)
            ztoks = [0] * HC
            oadd_toks = [0] * (HC * TGD)
            first_down = True
            wsrc = wd_d if first_down else wdbf_scr
            qn = 2816

            with ExitStack() as pctx:
                pz = [pctx.enter_context(nc.psum_tensor(f"pz{ga}_{gd}_{j}", [128, 512], F32))
                      for j in range(2)]
                ptz = [pctx.enter_context(nc.psum_tensor(f"ptz{ga}_{gd}_{j}", [128, 512], F32))
                       for j in range(4)]
                for hc in range(HC):
                    @block.gpsimd
                    def _(g, hc=hc):
                        if hc == 0:
                            bar(g)
                            if not first_down:
                                g.wait_ge(qWC.h, qWC.n)
                        for j in range(4):
                            if hc >= 1:
                                g.wait_ge(ePE.h, PE0 + (hc - 1) * pe_per + j + 1)
                            i = g.dma_start(out=wd_buf[:, hc * 0 + j * qn:(j + 1) * qn]
                                            if False else wd_buf[:, j * qn:(j + 1) * qn],
                                            in_=wsrc[hc, :, j * qn:(j + 1) * qn])
                            wdq[hc][j] = qWD[j].inc(i, 16)

                    @block.sync
                    def _(s, hc=hc):
                        if hc == 0:
                            bar(s)
                        if hc >= 2:
                            s.wait_ge(eDVE.h, ztoks[hc - 2])
                        i = s.dma_start(out=gz_st[hc % 2][:],
                                        in_=g_scr[ga, hc, :, gd * 512:(gd + 1) * 512])
                        gztoks[hc] = qGz[hc % 2].inc(i, 16)
                        for td in range(TGD):
                            k = hc * TGD + td
                            if k >= 4:
                                s.wait_ge(eDVE.h, oadd_toks[k - 4])
                            r0 = r_base + td * 128
                            i = s.dma_start(out=x_st[k % 4][:],
                                            in_=x_in[r0:r0 + 128, hc * 128:(hc + 1) * 128])
                            xctoks[k] = qXc[k % 4].inc(i, 16)
                        if False:
                            for j in range(4):
                                s.wait_ge(qWD[j].h, wdq[hc][j])
                            i = s.dma_start(out=wdbf_scr[hc], in_=wd_buf)
                            qWC.inc(i, 16)
                        if ita is not None:
                            ita.maybe_aload(s)

                    @block.tensor
                    def _(t, hc=hc):
                        if hc == 0:
                            bar(t)
                        if hc >= 2:
                            t.wait_ge(eDVE.h, ztoks[hc - 2])
                        for pc in range(NPC):
                            if pc % 22 == 0:
                                t.wait_ge(qWD[pc // 22].h, wdq[hc][pc // 22])
                            i = t.matmul(pz[hc % 2][:, 0:TGD * 128],
                                         lhsT=wd_buf[:, pc * 128:(pc + 1) * 128],
                                         rhs=arena[:, pc * (TGD * 128):(pc + 1) * (TGD * 128)],
                                         start=(pc == 0), stop=(pc == NPC - 1))
                            if pc % 22 == 21:
                                ePE.inc(i)

                    @block.vector
                    def _(v, hc=hc):
                        if hc == 0:
                            bar(v)
                        v.wait_ge(ePE.h, PE0 + hc * pe_per + 4)
                        v.wait_ge(qGz[hc % 2].h, gztoks[hc])
                        i = v.tensor_tensor(out=z_st[hc % 2][:],
                                            in0=pz[hc % 2][:, 0:TGD * 128],
                                            in1=gz_st[hc % 2][:, 0:TGD * 128],
                                            op=ALU.mult)
                        ztoks[hc] = eDVE.inc(i)
                        if ita is not None:
                            ita.emit_iter(v)

                    @block.tensor
                    def _(t, hc=hc):
                        t.wait_ge(eDVE.h, ztoks[hc])
                        for td in range(TGD):
                            k = hc * TGD + td
                            if k >= 4:
                                t.wait_ge(eDVE.h, oadd_toks[k - 4])
                            i = t.transpose(ptz[k % 4][:, 0:128],
                                            in_=z_st[hc % 2][:, td * 128:(td + 1) * 128],
                                            identity=idf_sb[:])
                            ePE.inc(i)

                    @block.vector
                    def _(v, hc=hc):
                        for td in range(TGD):
                            k = hc * TGD + td
                            v.wait_ge(ePE.h, PE0 + hc * pe_per + 4 + td + 1)
                            v.wait_ge(qXc[k % 4].h, xctoks[k])
                            if k >= 4:
                                v.wait_ge(qO[k % 4].h, otoks[k - 4])
                            i = v.tensor_tensor(out=o_st[k % 4][:],
                                                in0=ptz[k % 4][:, 0:128],
                                                in1=x_st[k % 4][:], op=ALU.add)
                            oadd_toks[k] = eDVE.inc(i)
                        if ita is not None:
                            ita.emit_iter(v)

                    @block.sync
                    def _(s, hc=hc):
                        for td in range(TGD):
                            k = hc * TGD + td
                            s.wait_ge(eDVE.h, oadd_toks[k])
                            r0 = r_base + td * 128
                            i = s.dma_start(out=out_d[r0:r0 + 128, hc * 128:(hc + 1) * 128],
                                            in_=o_st[k % 4][:])
                            otoks[k] = qO[k % 4].inc(i, 16)
                assert ePE.n == PE0 + HC * pe_per
            if ita is not None:
                ita.drain()
                ita.emit_act()
            take_snap()

        def dump_xn(ga):
            @block.sync
            def _(s):
                bar(s)
                i = s.dma_start(out=xn_dump[ga], in_=arena[:, 0:HC * GA])
                qD.inc(i, 16)
            take_snap()

        init_phase()
        for ga in range(n_ga):
            phase_ln(ga)
            if dump:
                dump_xn(ga)
            phase_gu(ga)
            phase_ffn(ga)
            ita0 = phase_sel_it(ga, 0, 0)
            a_free = phase_products(ga, 0, ita0.end_tok)
            if dump:
                @block.sync
                def _(s):
                    bar(s)
                    i = s.dma_start(out=p_dump[ga * n_gd], in_=arena[:, 0:NPC * TGD * 128])
                    qD.inc(i, 16)
                take_snap()
            if n_gd == 2:
                ita1 = ItA(ga, 1, a_free)
                phase_down(ga, 0, ita=ita1)
                a_free = phase_products(ga, 1, ita1.end_tok)
                phase_down(ga, 1, ita=None)
            else:
                phase_down(ga, 0, ita=None)

        @block.sync
        def _(s):
            for sm in sems.values():
                if sm.n > 0:
                    s.wait_ge(sm.h, sm.n)

        @block.gpsimd
        def _(g):
            for sm in sems.values():
                if sm.n > 0:
                    g.wait_ge(sm.h, sm.n)

    return nc



def _prep_weights(inputs):
    w_gate = np.ascontiguousarray(np.asarray(inputs["L0_attn_w_gate"], dtype=np.float32))
    w_up = np.ascontiguousarray(np.asarray(inputs["L0_attn_w_up"], dtype=np.float32))
    w_down = np.ascontiguousarray(np.asarray(inputs["L0_attn_w_down"], dtype=np.float32))
    W_base = np.ascontiguousarray(np.asarray(inputs["L0_ffn_W_base"], dtype=np.float32))
    b_base = np.asarray(inputs["L0_ffn_b_base"], dtype=np.float32)
    W_router = np.asarray(inputs["L0_ffn_W_router"], dtype=np.float32)
    A = np.asarray(inputs["L0_ffn_A"], dtype=np.float32)
    B = np.asarray(inputs["L0_ffn_B"], dtype=np.float32)

    def gu_img(w):
        wp = np.zeros((IP, H), np.float32)
        wp[:I_TRUE] = w
        t = wp.reshape(NIC, WI, HC, 128)
        return np.ascontiguousarray(t.transpose(0, 3, 2, 1)).reshape(NIC, 128, HC * WI)

    wg_img = gu_img(w_gate)
    wu_img = gu_img(w_up)

    wdp = np.zeros((H, IP), np.float32)
    wdp[:, :I_TRUE] = w_down
    t = wdp.reshape(HC, 128, NPC, 128)
    wd_img = np.ascontiguousarray(t.transpose(0, 3, 2, 1)).reshape(HC, 128, NPC * 128)

    t = W_base.reshape(HC, 128, HC, 128)
    wb_img = np.ascontiguousarray(t.transpose(0, 3, 2, 1)).reshape(HC, 128, HC * 128)

    A2 = A.reshape(128, H)
    a_img = np.ascontiguousarray(A2.T.reshape(HC, 128, 128).transpose(1, 0, 2)
                                 ).reshape(128, HC * 128)

    b_img = np.ascontiguousarray(B.transpose(0, 2, 1)).reshape(128, H)

    wr_img = np.ascontiguousarray(W_router.T.reshape(HC, 128, 8).transpose(1, 0, 2)
                                  ).reshape(128, HC * 8)
    bb_img = np.ascontiguousarray(b_base.reshape(HC, 128).T)

    identb = np.eye(128, dtype=ml_dtypes.bfloat16)
    identf = np.eye(128, dtype=np.float32)
    eexp = np.zeros((8, 128), dtype=ml_dtypes.bfloat16)
    for e in range(8):
        eexp[e, e * 16:(e + 1) * 16] = 1

    return dict(wg=wg_img, wu=wu_img, wd=wd_img, wb=wb_img, a_img=a_img,
                b_img=b_img, wr=wr_img, bb=bb_img, identb=identb,
                identf=identf, eexp=eexp)


def kernel(**inputs):
    x = np.asarray(inputs["L0_attn_x"], dtype=np.float32).reshape(16384, H)
    shared = _prep_weights(inputs)
    T_core = 16384 // N_CORES
    nc = build_nc(T_core)
    in_maps = []
    for c in range(N_CORES):
        m = dict(shared)
        m["x"] = np.ascontiguousarray(x[c * T_core:(c + 1) * T_core])
        in_maps.append(m)
    res = run_bass_kernel_spmd(nc, in_maps, core_ids=list(range(N_CORES)))
    out = np.concatenate([np.asarray(r["out"]) for r in res.results], axis=0)
    return out.reshape(1, 16384, H).astype(np.float32)
